# version 21
# baseline (speedup 1.0000x reference)
import math
from contextlib import ExitStack

import numpy as np
import ml_dtypes

import concourse.bass as bass
import concourse.mybir as mybir
from concourse.bass_utils import run_bass_kernel_spmd

F32 = mybir.dt.float32
BF16 = mybir.dt.bfloat16
F8 = mybir.dt.float8e4
ALU = mybir.AluOpType
ACTF = mybir.ActivationFunctionType
AX = mybir.AxisListType

EPS = 1e-5
FP8MAX = 448.0
ISQ = 1.0 / math.sqrt(128.0)
BCLAMP = 44.0

ENGINES = ("PE", "ACT", "DVE", "SP", "POOL")


class Em:

    def __init__(self, nc):
        self.nc = nc
        self.sems = {}
        self.cnt = {}
        self.clock = {e: {} for e in ENGINES}
        self.snap = {}
        self.lw = {}
        self.rd = {}
        self.stream = {e: [] for e in ENGINES}
        self.fence = {}
        for e in ENGINES:
            self._sem(e)

    def _sem(self, dim):
        if dim not in self.sems:
            self.sems[dim] = self.nc.alloc_semaphore("em_" + dim)
            self.cnt[dim] = 0
        return self.sems[dim]

    def _waits(self, eng, reads, writes):
        need = {}

        def add(dt):
            if dt is None:
                return
            d, t = dt
            if need.get(d, 0) < t:
                need[d] = t

        for r in reads:
            add(self.lw.get(r))
        for w in writes:
            add(self.lw.get(w))
            for d, t in self.rd.get(w, {}).items():
                add((d, t))
        for d, t in self.fence.items():
            add((d, t))
        out = []
        c = self.clock[eng]
        for d in sorted(need):
            t = need[d]
            if c.get(d, 0) >= t:
                continue
            if d in ENGINES:
                st = t
                val = t
            else:
                st = self.cnt[d]
                val = 16 * st
            sn = self.snap.get((d, st), {})
            for d2, t2 in sn.items():
                if c.get(d2, 0) < t2:
                    c[d2] = t2
            if c.get(d, 0) < st:
                c[d] = st
            out.append((self.sems[d], val))
        return out

    def _update(self, reads, writes, dim, t):
        for r in reads:
            self.rd.setdefault(r, {})[dim] = t
        for w in writes:
            self.lw[w] = (dim, t)
            self.rd[w] = {}

    def op(self, eng, fn, reads=(), writes=()):
        waits = self._waits(eng, reads, writes)
        self.cnt[eng] += 1
        t = self.cnt[eng]
        self.snap[(eng, t)] = dict(self.clock[eng], **{eng: t})
        self._update(reads, writes, eng, t)
        sem = self.sems[eng]

        def emit(e, waits=waits, fn=fn, sem=sem):
            for s, v in waits:
                e.wait_ge(s, v)
            fn(e).then_inc(sem, 1)

        self.stream[eng].append(emit)
        return t

    def dma(self, queue, sdim, out_ap, in_ap, reads=(), writes=()):
        self._sem(sdim)
        waits = self._waits(queue, reads, writes)
        self.cnt[sdim] += 1
        t = self.cnt[sdim]
        self.snap[(sdim, t)] = dict(self.clock[queue], **{sdim: t})
        self._update(reads, writes, sdim, t)
        sem = self.sems[sdim]

        def emit(e, waits=waits, out_ap=out_ap, in_ap=in_ap, sem=sem):
            for s, v in waits:
                e.wait_ge(s, v)
            e.dma_start(out=out_ap, in_=in_ap).then_inc(sem, 16)

        self.stream[queue].append(emit)
        return t

    def free_regions(self, keys):
        for k in keys:
            dt = self.lw.pop(k, None)
            if dt is not None:
                d, t = dt
                if self.fence.get(d, 0) < t:
                    self.fence[d] = t
            for d, t in self.rd.pop(k, {}).items():
                if self.fence.get(d, 0) < t:
                    self.fence[d] = t

    def raw(self, eng, fn):
        self.stream[eng].append(fn)

    def finalize(self):
        with self.nc.Block() as block:
            for name, reg in (("SP", block.sync), ("PE", block.tensor),
                              ("DVE", block.vector), ("ACT", block.scalar),
                              ("POOL", block.gpsimd)):
                fns = self.stream[name]
                if fns:
                    reg(lambda e, fns=fns: [f(e) for f in fns])


def build_program(D=2048, S=2048, verbose=False):
    NCH = D // 128
    H = NCH
    NT = S // 512
    QC = S // 128
    HW = S // 2
    nc = bass.Bass("TRN2", target_bir_lowering=False, dynamic_dma_scratch_size=8192)
    em = Em(nc)

    x_d = nc.dram_tensor("x", [D, S], F32, kind="ExternalInput").ap()
    wqkv_d = nc.dram_tensor("wqkv", [H, 128, 3 * NCH * 128], F32, kind="ExternalInput").ap()
    wo_d = nc.dram_tensor("wo", [NCH, 128, NCH * 128], F32, kind="ExternalInput").ap()
    wf_d = nc.dram_tensor("wf", [NCH, 128, NCH * 128], F8, kind="ExternalInput").ap()
    ss_d = nc.dram_tensor("sscale", [128, 1], BF16, kind="ExternalInput").ap()
    ws_d = nc.dram_tensor("wscale", [128, 1], F32, kind="ExternalInput").ap()
    out_d = nc.dram_tensor("out", [D, S], F32, kind="ExternalOutput").ap()
    spR_d = nc.dram_tensor("spill_r", [NCH, 128, S], F32, kind="Internal").ap()
    spX_d = nc.dram_tensor("spill_x3", [NCH, 128, S], F32, kind="Internal").ap()

    st = ExitStack()
    A = lambda name, shape, dtype: st.enter_context(nc.sbuf_tensor(name, shape, dtype))

    ones = A("ones", [128, 128], BF16)
    ssc = A("ssc", [128, 1], BF16)
    wsc = A("wsc", [128, 1], F32)
    s32 = A("s32", [128, 1], F32)
    inv_s = A("inv_s", [128, 1], F32)
    combo = A("combo", [128, 1], F32)
    m_b = A("m_b", [128, S], F32)
    rs_b = A("rs_b", [128, S], F32)
    ln_t = A("ln_t", [128, S], F32)
    P = [st.enter_context(nc.psum_tensor(f"P{i}", [128, 1024], F32)) for i in range(4)]
    PK = [f"P{i}" for i in range(4)]
    TSL = [slice(i * 512, (i + 1) * 512) for i in range(NT)]

    def pslot(s):
        ti = s % 4
        half = (s // 4) % 2
        return ti, slice(half * 512, half * 512 + 512)

    def grp_slices(j):
        base = 0 if j == 0 else 2
        out = []
        for t in range(NT):
            out.append((base + t // 2, slice((t % 2) * 512, (t % 2) * 512 + 512)))
        return out

    em.op("DVE", lambda e: e.memset(ones[:], 1.0), writes=["ones"])
    em.dma("SP", "dsc", ssc[:], ss_d, writes=["ssc"])
    em.dma("SP", "dsc", wsc[:], ws_d, writes=["wsc"])
    em.op("DVE", lambda e: e.tensor_copy(s32[:], ssc[:]), reads=["ssc"], writes=["s32"])
    em.op("DVE", lambda e: e.reciprocal(inv_s[:], s32[:]), reads=["s32"], writes=["inv_s"])
    em.op("DVE", lambda e: e.tensor_mul(combo[:], s32[:], wsc[:]), reads=["s32", "wsc"], writes=["combo"])
    em.op("DVE", lambda e: e.tensor_scalar(combo[:], combo[:], 0.5, None, ALU.mult),
          reads=["combo"], writes=["combo"])

    def emit_stats(Rt, Rk, fold_inv_s, ca, sq, cak, sqk, pre=None):
        if pre is not None:
            pre(0)
            if NCH > 1:
                pre(1)
        for c in range(NCH):
            em.op("ACT", lambda e, c=c: e.copy(ca[:], Rt[c][:]), reads=[Rk[c]], writes=[cak])
            em.op("DVE", lambda e: e.tensor_mul(sq[:], ca[:], ca[:]), reads=[cak], writes=[sqk])

            def mm_sum(e, c=c):
                last = None
                for t in range(NT):
                    ti, sl = grp_slices(0)[t]
                    last = e.matmul(P[ti][:, sl], ones[:], ca[:, TSL[t]],
                                    start=(c == 0), stop=(c == NCH - 1))
                return last
            em.op("PE", mm_sum, reads=[cak, "ones"], writes=[PK[0], PK[1]])

            def mm_sq(e, c=c):
                last = None
                for t in range(NT):
                    ti, sl = grp_slices(1)[t]
                    last = e.matmul(P[ti][:, sl], ones[:], sq[:, TSL[t]],
                                    start=(c == 0), stop=(c == NCH - 1))
                return last
            em.op("PE", mm_sq, reads=[sqk, "ones"], writes=[PK[2], PK[3]])
            if pre is not None and c + 2 < NCH:
                pre(c + 2)
        inv_d = 1.0 / D
        for t in range(NT):
            ti, sl = grp_slices(0)[t]
            em.op("DVE", lambda e, t=t, ti=ti, sl=sl:
                  e.tensor_scalar(m_b[:, TSL[t]], P[ti][:, sl], inv_d, None, ALU.mult),
                  reads=[PK[ti]], writes=["m_b"])
            ti2, sl2 = grp_slices(1)[t]
            em.op("DVE", lambda e, t=t, ti2=ti2, sl2=sl2:
                  e.tensor_scalar(ln_t[:, TSL[t]], P[ti2][:, sl2], inv_d, None, ALU.mult),
                  reads=[PK[ti2]], writes=["ln_t"])
        em.op("DVE", lambda e: e.tensor_mul(rs_b[:], m_b[:], m_b[:]), reads=["m_b"], writes=["rs_b"])
        em.op("DVE", lambda e: e.tensor_sub(ln_t[:], ln_t[:], rs_b[:]), reads=["ln_t", "rs_b"], writes=["ln_t"])
        em.op("DVE", lambda e: e.tensor_scalar(ln_t[:], ln_t[:], EPS, None, ALU.add),
              reads=["ln_t"], writes=["ln_t"])
        em.op("ACT", lambda e: e.activation(ln_t[:], ln_t[:], ACTF.Ln), reads=["ln_t"], writes=["ln_t"])
        em.op("ACT", lambda e: e.activation(rs_b[:], ln_t[:], ACTF.Exp, scale=-0.5),
              reads=["ln_t"], writes=["rs_b"])
        if fold_inv_s:
            em.op("DVE", lambda e: e.tensor_scalar(rs_b[:], rs_b[:], inv_s[:], None, ALU.mult),
                  reads=["rs_b", "inv_s"], writes=["rs_b"])

    def emit_ffn(Rt, Rk, xq, xqk, ap, apk, wfb, wfk, wf_sems, to_out=False,
                 post_final=None):
        for c in range(NCH):
            em.op("DVE", lambda e, c=c: e.tensor_sub(ap[:], Rt[c][:], m_b[:]),
                  reads=[Rk[c], "m_b"], writes=[apk])
            em.op("DVE", lambda e: e.tensor_mul(ap[:], ap[:], rs_b[:]),
                  reads=[apk, "rs_b"], writes=[apk])
            em.op("POOL", lambda e: e.tensor_scalar(ap[:], ap[:], FP8MAX, -FP8MAX,
                                                    ALU.min, ALU.max),
                  reads=[apk], writes=[apk])
            em.op("ACT", lambda e, c=c: e.copy(xq[c // 2][:, c % 2, :], ap[:]),
                  reads=[apk], writes=[xqk[c]])
        for o in range(NCH):
            j = o % 2
            em.dma("SP", wf_sems[j], wfb[j][:], wf_d[o], writes=[wfk[j]])
            sls = grp_slices(j)
            gk = sorted({PK[ti] for ti, _ in sls})

            def mms(e, j=j, sls=sls):
                last = None
                for i in range(NCH // 2):
                    for t in range(NT):
                        ti, sl = sls[t]
                        last = e.matmul(P[ti][:, sl],
                                        wfb[j][:, 2 * i * 128:(2 * i + 2) * 128]
                                        .rearrange("p (k d) -> p k d", k=2),
                                        xq[i][:, :, TSL[t]],
                                        start=(i == 0), stop=(i == NCH // 2 - 1),
                                        perf_mode=mybir.MatmulPerfMode.DoubleRow)
                return last
            em.op("PE", mms, reads=[wfk[j]] + list(xqk), writes=gk)
            for t in range(NT):
                ti, sl = sls[t]
                if not to_out:
                    em.op("DVE", lambda e, o=o, t=t, ti=ti, sl=sl:
                          e.scalar_tensor_tensor(Rt[o][:, TSL[t]], P[ti][:, sl], combo[:],
                                                 Rt[o][:, TSL[t]], ALU.mult, ALU.add),
                          reads=[PK[ti], "combo", Rk[o]], writes=[Rk[o]])
                else:
                    em.op("DVE", lambda e, o=o, t=t, ti=ti, sl=sl:
                          e.scalar_tensor_tensor(ap[:, TSL[t]], P[ti][:, sl], combo[:],
                                                 Rt[o][:, TSL[t]], ALU.mult, ALU.add),
                          reads=[PK[ti], "combo", Rk[o]], writes=[apk])
            if to_out:
                em.dma("ACT", "dout", out_d[o * 128:(o + 1) * 128, :], ap[:], reads=[apk])
            if post_final is not None:
                post_final(o)

    s1 = ExitStack()
    A1 = lambda name, shape, dtype: s1.enter_context(nc.sbuf_tensor(name, shape, dtype))
    R = [A1(f"R{c}", [128, S], F32) for c in range(NCH)]
    RK = [f"R{c}" for c in range(NCH)]
    ca1 = A1("ca1", [128, S], BF16)
    sq1 = A1("sq1", [128, S], BF16)
    xq1 = [A1(f"xq{c}", [128, 2, S], F8) for c in range(NCH // 2)]
    XQ1K = [f"xq{c}" for c in range(NCH)]
    ap1 = A1("ap1", [128, S], F32)
    wfb1 = [A1(f"wfb{j}", [128, NCH * 128], F8) for j in range(2)]
    WF1K = ["wfb0", "wfb1"]

    def pre_x(c):
        em.dma("SP", f"dx{c}", R[c][:], x_d[c * 128:(c + 1) * 128, :], writes=[RK[c]])
    emit_stats(R, RK, True, ca1, sq1, "ca1", "sq1", pre=pre_x)

    def spill_r(o):
        em.dma("ACT", "dsp", spR_d[o], R[o][:], reads=[RK[o]], writes=[f"spR{o}"])
    emit_ffn(R, RK, xq1, XQ1K, ap1, "ap1", wfb1, WF1K, ("wf1A", "wf1B"),
             post_final=spill_r)
    emit_stats(R, RK, False, ca1, sq1, "ca1", "sq1")
    em.free_regions(RK + XQ1K + WF1K + ["ca1", "sq1", "ap1"])
    s1.close()

    s2 = ExitStack()
    A2 = lambda name, shape, dtype: s2.enter_context(nc.sbuf_tensor(name, shape, dtype))
    u = [A2(f"u{c}", [128, S], BF16) for c in range(NCH)]
    UK = [f"u{c}" for c in range(NCH)]
    ln2 = [A2(f"ln2_{c}", [128, S], BF16) for c in range(NCH)]
    LNK = [f"ln2_{c}" for c in range(NCH)]

    s2a = ExitStack()
    rb = [s2a.enter_context(nc.sbuf_tensor(f"rb{j}", [128, S], F32)) for j in range(2)]
    RBK = ["rb0", "rb1"]
    rc16 = [s2a.enter_context(nc.sbuf_tensor(f"rc{j}", [128, S], BF16)) for j in range(2)]
    RCK = ["rc0", "rc1"]
    rsbf = s2a.enter_context(nc.sbuf_tensor("rsbf", [128, S], BF16))
    em.op("DVE", lambda e: e.tensor_copy(rsbf[:], rs_b[:]), reads=["rs_b"], writes=["rsbf"])
    for c in range(NCH):
        j = c % 2
        em.dma("SP", f"dr{j}", rb[j][:], spR_d[c], reads=[f"spR{c}"], writes=[RBK[j]])
        em.op("DVE", lambda e, j=j: e.tensor_sub(rc16[j][:], rb[j][:], m_b[:]),
              reads=[RBK[j], "m_b"], writes=[RCK[j]])
        em.op("DVE", lambda e, c=c, j=j: e.tensor_mul(ln2[c][:], rc16[j][:], rsbf[:]),
              reads=[RCK[j], "rsbf"], writes=[LNK[c]])
    em.free_regions(RBK + RCK + ["rsbf"])
    s2a.close()

    s2b = ExitStack()
    A2b = lambda name, shape, dtype: s2b.enter_context(nc.sbuf_tensor(name, shape, dtype))
    qs2 = [A2b(f"qs{j}", [128, S], BF16) for j in range(2)]
    ks2 = [A2b(f"ks{j}", [128, S], BF16) for j in range(2)]
    QSK = ["qs0", "qs1"]
    KSK = ["ks0", "ks1"]
    vs = A2b("vs", [128, S], BF16)
    es2 = [A2b(f"es{j}", [128, S], BF16) for j in range(3)]
    ESK = ["es0", "es1", "es2"]
    whb = [A2b(f"whb{j}", [128, 3 * NCH * 128], BF16) for j in range(2)]
    WHK = ["whb0", "whb1"]
    bias2 = [A2b(f"bias{j}", [128, 1], F32) for j in range(2)]
    BIK = ["bias0", "bias1"]
    am2 = [A2b(f"am{j}", [128, 2], BF16) for j in range(2)]
    AMK = ["am0", "am1"]
    mm2 = [A2b(f"mm{j}", [128, 2], F32) for j in range(2)]
    MMK = ["mm0", "mm1"]
    rp2 = [A2b(f"rp{j}", [128, 2], F32) for j in range(3)]
    RPK = ["rp0", "rp1", "rp2"]
    rv2 = [A2b(f"rv{j}", [128, 1], F32) for j in range(3)]
    RVK = ["rv0", "rv1", "rv2"]
    rl2 = [A2b(f"rl{j}", [128, 128], BF16) for j in range(3)]
    RLK = ["rl0", "rl1", "rl2"]

    slot_ctr = [0]

    def next_slot():
        s = slot_ctr[0]
        slot_ctr[0] += 1
        return pslot(s)

    def emit_qkv_mm(g, dst, dstk, whk, jb, pieces=False):
        for t in range(NT):
            ti, sl = next_slot()
            if pieces:
                for i in range(NCH):
                    def piece(e, t=t, g=g, jb=jb, ti=ti, sl=sl, i=i):
                        return e.matmul(P[ti][:, sl],
                                        whb[jb][:, (g * NCH + i) * 128:(g * NCH + i + 1) * 128],
                                        ln2[i][:, TSL[t]], start=(i == 0), stop=(i == NCH - 1))
                    em.op("PE", piece, reads=[whk, LNK[i]], writes=[PK[ti]])
            else:
                def mms(e, t=t, g=g, jb=jb, ti=ti, sl=sl):
                    last = None
                    for i in range(NCH):
                        last = e.matmul(P[ti][:, sl],
                                        whb[jb][:, (g * NCH + i) * 128:(g * NCH + i + 1) * 128],
                                        ln2[i][:, TSL[t]], start=(i == 0), stop=(i == NCH - 1))
                    return last
                em.op("PE", mms, reads=[whk] + LNK, writes=[PK[ti]])
            em.op("DVE", lambda e, t=t, ti=ti, sl=sl, dst=dst:
                  e.tensor_copy(dst[:, TSL[t]], P[ti][:, sl]),
                  reads=[PK[ti]], writes=[dstk])

    HSL = [slice(0, HW), slice(HW, S)]

    def emit_scores_half(qc, hf, jb):
        def mms(e, qc=qc, hf=hf, jb=jb):
            last = None
            off = HSL[hf].start
            n = 0
            while n < HW:
                w = min(512, HW - n)
                last = e.matmul(P[hf][:, n:n + w], qs2[jb][:, qc * 128:(qc + 1) * 128],
                                ks2[jb][:, off + n:off + n + w], start=True, stop=True)
                n += w
            return last
        em.op("PE", mms, reads=[QSK[jb], KSK[jb]], writes=[PK[hf]])

    def emit_colsum(qc, je):
        def mms(e, qc=qc, je=je):
            last = None
            for t in range(NT):
                ti, sl = grp_slices(1)[t]
                last = e.matmul(P[ti][:, sl], rl2[je][:], es2[je][:, TSL[t]],
                                start=(qc == 0), stop=(qc == QC - 1))
            return last
        em.op("PE", mms, reads=[RLK[je], ESK[je] + "_h0", ESK[je] + "_h1"],
              writes=[PK[2], PK[3]])

    for h in range(H):
        jb = h % 2
        em.dma("POOL", f"wh{jb}", whb[jb][:], wqkv_d[h], writes=[WHK[jb]])
        slot_ctr[0] = 0
        emit_qkv_mm(0, qs2[jb], QSK[jb], WHK[jb], jb)
        em.op("DVE", lambda e, jb=jb: e.tensor_reduce(am2[jb][:, 0:1], qs2[jb][:], AX.X,
                                                      ALU.max, apply_absolute_value=True),
              reads=[QSK[jb]], writes=[AMK[jb]])
        emit_qkv_mm(1, ks2[jb], KSK[jb], WHK[jb], jb)
        em.op("DVE", lambda e, jb=jb: e.tensor_reduce(am2[jb][:, 1:2], ks2[jb][:], AX.X,
                                                      ALU.max, apply_absolute_value=True),
              reads=[KSK[jb]], writes=[AMK[jb]])
        em.op("DVE", lambda e, jb=jb: e.tensor_mul(am2[jb][:], am2[jb][:], am2[jb][:]),
              reads=[AMK[jb]], writes=[AMK[jb]])
        emit_qkv_mm(2, vs, "vs", WHK[jb], jb)
        em.op("PE", lambda e, jb=jb: e.matmul(P[3][:, 512:514], ones[:], am2[jb][:],
                                              start=True, stop=True),
              reads=[AMK[jb], "ones"], writes=[PK[3]])
        em.op("DVE", lambda e, jb=jb: e.tensor_copy(mm2[jb][:], P[3][:, 512:514]),
              reads=[PK[3]], writes=[MMK[jb]])
        em.op("DVE", lambda e, jb=jb: e.tensor_mul(mm2[jb][:, 0:1], mm2[jb][:, 0:1],
                                                   mm2[jb][:, 1:2]),
              reads=[MMK[jb]], writes=[MMK[jb]])
        em.op("ACT", lambda e, jb=jb: e.activation(mm2[jb][:, 0:1], mm2[jb][:, 0:1], ACTF.Ln),
              reads=[MMK[jb]], writes=[MMK[jb]])
        em.op("ACT", lambda e, jb=jb: e.activation(bias2[jb][:], mm2[jb][:, 0:1], ACTF.Exp, scale=0.5),
              reads=[MMK[jb]], writes=[BIK[jb]])
        em.op("DVE", lambda e, jb=jb: e.tensor_scalar(bias2[jb][:], bias2[jb][:], -ISQ, -BCLAMP,
                                                      ALU.mult, ALU.max),
              reads=[BIK[jb]], writes=[BIK[jb]])
        for qc in range(QC):
            je = qc % 3
            emit_scores_half(qc, 0, jb)
            em.op("ACT", lambda e, je=je, jb=jb:
                  e.activation(es2[je][:, HSL[0]], P[0][:, 0:HW], ACTF.Exp,
                               bias=bias2[jb][:], scale=ISQ),
                  reads=[PK[0], BIK[jb]], writes=[ESK[je] + "_h0"])
            emit_scores_half(qc, 1, jb)
            em.op("DVE", lambda e, je=je: e.tensor_reduce(rp2[je][:, 0:1], es2[je][:, HSL[0]],
                                                          AX.X, ALU.add),
                  reads=[ESK[je] + "_h0"], writes=[RPK[je] + "_a"])
            em.op("ACT", lambda e, je=je, jb=jb:
                  e.activation(es2[je][:, HSL[1]], P[1][:, 0:HW], ACTF.Exp,
                               bias=bias2[jb][:], scale=ISQ,
                               accum_out=rp2[je][:, 1:2]),
                  reads=[PK[1], BIK[jb]], writes=[ESK[je] + "_h1", RPK[je] + "_b"])
            em.op("DVE", lambda e, je=je: e.tensor_reduce(rv2[je][:], rp2[je][:], AX.X, ALU.add),
                  reads=[RPK[je] + "_a", RPK[je] + "_b"], writes=[RVK[je]])
            em.op("DVE", lambda e, je=je: e.reciprocal(rv2[je][:], rv2[je][:]),
                  reads=[RVK[je]], writes=[RVK[je]])
            em.op("DVE", lambda e, je=je: e.tensor_scalar(rl2[je][:], ones[:], rv2[je][:],
                                                          None, ALU.mult),
                  reads=["ones", RVK[je]], writes=[RLK[je]])
            if qc >= 1:
                emit_colsum(qc - 1, (qc - 1) % 3)
        emit_colsum(QC - 1, (QC - 1) % 3)
        for p in range((NT + 1) // 2):
            w = min(1024, S - p * 1024)
            em.op("DVE", lambda e, h=h, p=p, w=w:
                  e.tensor_mul(u[h][:, p * 1024:p * 1024 + w], vs[:, p * 1024:p * 1024 + w],
                               P[2 + p][:, 0:w]),
                  reads=["vs", PK[2 + p]], writes=[UK[h]])
    em.free_regions(QSK + KSK + [k + s for k in ESK for s in ("_h0", "_h1")] +
                    WHK + BIK + [k + s for k in RPK for s in ("_a", "_b")] + RVK + RLK +
                    AMK + MMK + ["vs"])
    s2b.close()

    s3 = ExitStack()
    A3 = lambda name, shape, dtype: s3.enter_context(nc.sbuf_tensor(name, shape, dtype))
    wob = [A3(f"wob{j}", [128, NCH * 128], BF16) for j in range(2)]
    WOK = ["wob0", "wob1"]
    rr = [A3(f"rr{j}", [128, S], F32) for j in range(2)]
    RRK = ["rr0", "rr1"]
    stg = [A3(f"stg{j}", [128, S], F32) for j in range(2)]
    STK = ["stg0", "stg1"]
    for o in range(NCH):
        j = o % 2
        em.dma("POOL", f"dwo{j}", wob[j][:], wo_d[o], writes=[WOK[j]])
        em.dma("SP", f"drr{j}", rr[j][:], spR_d[o], reads=[f"spR{o}"], writes=[RRK[j]])
        sls = grp_slices(j)

        def mms(e, j=j, sls=sls):
            last = None
            for hd in range(NCH):
                for t in range(NT):
                    ti, sl = sls[t]
                    last = e.matmul(P[ti][:, sl], wob[j][:, hd * 128:(hd + 1) * 128],
                                    u[hd][:, TSL[t]], start=(hd == 0), stop=(hd == NCH - 1))
            return last
        em.op("PE", mms, reads=[WOK[j]] + UK, writes=sorted({PK[ti] for ti, _ in sls}))
        for t in range(NT):
            ti, sl = sls[t]
            em.op("DVE", lambda e, t=t, j=j, ti=ti, sl=sl:
                  e.tensor_add(stg[j][:, TSL[t]], P[ti][:, sl], rr[j][:, TSL[t]]),
                  reads=[PK[ti], RRK[j]], writes=[STK[j]])
        em.dma("ACT", "dx3", spX_d[o], stg[j][:], reads=[STK[j]], writes=[f"spX{o}"])
    em.free_regions(WOK + RRK + STK)
    s3.close()
    em.free_regions(UK + LNK)
    s2.close()

    s4 = ExitStack()
    A4 = lambda name, shape, dtype: s4.enter_context(nc.sbuf_tensor(name, shape, dtype))
    R2 = [A4(f"R2_{c}", [128, S], F32) for c in range(NCH)]
    R2K = [f"R2_{c}" for c in range(NCH)]
    ca2 = A4("ca2", [128, S], BF16)
    sq2 = A4("sq2", [128, S], BF16)
    xq2 = [A4(f"xr{c}", [128, 2, S], F8) for c in range(NCH // 2)]
    XQ2K = [f"xr{c}" for c in range(NCH)]
    ap2 = A4("ap2", [128, S], F32)
    wfb2 = [A4(f"wg{j}", [128, NCH * 128], F8) for j in range(2)]
    WF2K = ["wg0", "wg1"]

    def pre_x3(c):
        em.dma("SP", f"dx3r{c}", R2[c][:], spX_d[c], reads=[f"spX{c}"], writes=[R2K[c]])
    emit_stats(R2, R2K, True, ca2, sq2, "ca2", "sq2", pre=pre_x3)
    emit_ffn(R2, R2K, xq2, XQ2K, ap2, "ap2", wfb2, WF2K, ("wf2A", "wf2B"),
             to_out=True)
    em.free_regions(R2K + XQ2K + WF2K + ["ca2", "sq2", "ap2"])
    s4.close()

    out_sem = em.sems["dout"]
    n_out = em.cnt["dout"]
    em.raw("SP", lambda e: e.wait_ge(out_sem, 16 * n_out))

    if verbose:
        print("sbuf bytes remaining:", nc.sbuf_bytes_remaining)
    em.finalize()
    st.close()
    return nc


def _host_layout(x_b, w_qkv, w_o, w_ffn, w_scale, in_scale, D, S):
    NCH = D // 128
    H = NCH
    wqkv = np.ascontiguousarray(
        w_qkv.reshape(NCH, 128, 3, H, 128).transpose(3, 1, 2, 0, 4).reshape(H, 128, 3 * NCH * 128))
    wo = np.ascontiguousarray(
        w_o.reshape(NCH, 128, NCH, 128).transpose(2, 1, 0, 3).reshape(NCH, 128, NCH * 128))
    wf = np.ascontiguousarray(
        w_ffn.reshape(NCH, 128, NCH, 128).transpose(0, 3, 2, 1).reshape(NCH, 128, NCH * 128))
    wf = wf.view(ml_dtypes.float8_e4m3)
    ss = np.ascontiguousarray(np.broadcast_to(np.asarray(in_scale).reshape(1, 1), (128, 1)))
    ws = np.ascontiguousarray(np.broadcast_to(np.asarray(w_scale).reshape(1, 1), (128, 1)))
    return {"x": np.ascontiguousarray(x_b.T), "wqkv": wqkv, "wo": wo, "wf": wf,
            "sscale": ss, "wscale": ws}


def kernel(L0_attn_x, L0_attn_w_qkv, L0_attn_w_o, L0_ffn_weight,
           L0_ffn_weight_scale, L0_ffn_input_scale):
    D, S = 2048, 2048
    x = np.asarray(L0_attn_x)
    w_qkv = np.asarray(L0_attn_w_qkv)
    w_o = np.asarray(L0_attn_w_o)
    w_ffn = np.asarray(L0_ffn_weight)
    nc = build_program(D, S)
    in_maps = [
        _host_layout(x[b], w_qkv, w_o, w_ffn, L0_ffn_weight_scale, L0_ffn_input_scale, D, S)
        for b in range(8)
    ]
    res = run_bass_kernel_spmd(nc, in_maps, core_ids=list(range(8)))
    out = np.stack([np.asarray(r["out"]).T for r in res.results], axis=0)
    return out.astype(np.float32)


# version 22
# speedup vs baseline: 1.1777x; 1.1777x over previous
import math
from contextlib import ExitStack

import numpy as np
import ml_dtypes

import concourse.bass as bass
import concourse.mybir as mybir
from concourse.bass_utils import run_bass_kernel_spmd

F32 = mybir.dt.float32
BF16 = mybir.dt.bfloat16
F8 = mybir.dt.float8e4
ALU = mybir.AluOpType
ACTF = mybir.ActivationFunctionType
AX = mybir.AxisListType

EPS = 1e-5
FP8MAX = 448.0
ISQ = 1.0 / math.sqrt(128.0)
BCLAMP = 44.0

ENGINES = ("PE", "ACT", "DVE", "SP", "POOL")


class Em:

    def __init__(self, nc):
        self.nc = nc
        self.sems = {}
        self.cnt = {}
        self.clock = {e: {} for e in ENGINES}
        self.snap = {}
        self.lw = {}
        self.rd = {}
        self.stream = {e: [] for e in ENGINES}
        self.fence = {}
        for e in ENGINES:
            self._sem(e)

    def _sem(self, dim):
        if dim not in self.sems:
            self.sems[dim] = self.nc.alloc_semaphore("em_" + dim)
            self.cnt[dim] = 0
        return self.sems[dim]

    def _waits(self, eng, reads, writes):
        need = {}

        def add(dt):
            if dt is None:
                return
            d, t = dt
            if need.get(d, 0) < t:
                need[d] = t

        for r in reads:
            add(self.lw.get(r))
        for w in writes:
            add(self.lw.get(w))
            for d, t in self.rd.get(w, {}).items():
                add((d, t))
        for d, t in self.fence.items():
            add((d, t))
        out = []
        c = self.clock[eng]
        for d in sorted(need):
            t = need[d]
            if c.get(d, 0) >= t:
                continue
            if d in ENGINES:
                st = t
                val = t
            else:
                st = self.cnt[d]
                val = 16 * st
            sn = self.snap.get((d, st), {})
            for d2, t2 in sn.items():
                if c.get(d2, 0) < t2:
                    c[d2] = t2
            if c.get(d, 0) < st:
                c[d] = st
            out.append((self.sems[d], val))
        return out

    def _update(self, reads, writes, dim, t):
        for r in reads:
            self.rd.setdefault(r, {})[dim] = t
        for w in writes:
            self.lw[w] = (dim, t)
            self.rd[w] = {}

    def op(self, eng, fn, reads=(), writes=()):
        waits = self._waits(eng, reads, writes)
        self.cnt[eng] += 1
        t = self.cnt[eng]
        self.snap[(eng, t)] = dict(self.clock[eng], **{eng: t})
        self._update(reads, writes, eng, t)
        sem = self.sems[eng]

        def emit(e, waits=waits, fn=fn, sem=sem):
            for s, v in waits:
                e.wait_ge(s, v)
            fn(e).then_inc(sem, 1)

        self.stream[eng].append(emit)
        return t

    def dma(self, queue, sdim, out_ap, in_ap, reads=(), writes=()):
        self._sem(sdim)
        waits = self._waits(queue, reads, writes)
        self.cnt[sdim] += 1
        t = self.cnt[sdim]
        self.snap[(sdim, t)] = dict(self.clock[queue], **{sdim: t})
        self._update(reads, writes, sdim, t)
        sem = self.sems[sdim]

        def emit(e, waits=waits, out_ap=out_ap, in_ap=in_ap, sem=sem):
            for s, v in waits:
                e.wait_ge(s, v)
            e.dma_start(out=out_ap, in_=in_ap).then_inc(sem, 16)

        self.stream[queue].append(emit)
        return t

    def free_regions(self, keys):
        for k in keys:
            dt = self.lw.pop(k, None)
            if dt is not None:
                d, t = dt
                if self.fence.get(d, 0) < t:
                    self.fence[d] = t
            for d, t in self.rd.pop(k, {}).items():
                if self.fence.get(d, 0) < t:
                    self.fence[d] = t

    def raw(self, eng, fn):
        self.stream[eng].append(fn)

    def finalize(self):
        with self.nc.Block() as block:
            for name, reg in (("SP", block.sync), ("PE", block.tensor),
                              ("DVE", block.vector), ("ACT", block.scalar),
                              ("POOL", block.gpsimd)):
                fns = self.stream[name]
                if fns:
                    reg(lambda e, fns=fns: [f(e) for f in fns])


def build_program(D=2048, S=2048, verbose=False):
    NCH = D // 128
    H = NCH
    NT = S // 512
    QC = S // 128
    HW = S // 2
    nc = bass.Bass("TRN2", target_bir_lowering=False, dynamic_dma_scratch_size=8192)
    em = Em(nc)

    x_d = nc.dram_tensor("x", [D, S], F32, kind="ExternalInput").ap()
    wqkv_d = nc.dram_tensor("wqkv", [H, 128, 3 * NCH * 128], F32, kind="ExternalInput").ap()
    wo_d = nc.dram_tensor("wo", [NCH, 128, NCH * 128], F32, kind="ExternalInput").ap()
    wf_d = nc.dram_tensor("wf", [NCH, 128, NCH * 128], F8, kind="ExternalInput").ap()
    ss_d = nc.dram_tensor("sscale", [128, 1], BF16, kind="ExternalInput").ap()
    ws_d = nc.dram_tensor("wscale", [128, 1], F32, kind="ExternalInput").ap()
    out_d = nc.dram_tensor("out", [D, S], F32, kind="ExternalOutput").ap()
    spR_d = nc.dram_tensor("spill_r", [NCH, 128, S], F32, kind="Internal").ap()
    spX_d = nc.dram_tensor("spill_x3", [NCH, 128, S], F32, kind="Internal").ap()

    st = ExitStack()
    A = lambda name, shape, dtype: st.enter_context(nc.sbuf_tensor(name, shape, dtype))

    ones = A("ones", [128, 128], BF16)
    ssc = A("ssc", [128, 1], BF16)
    wsc = A("wsc", [128, 1], F32)
    s32 = A("s32", [128, 1], F32)
    inv_s = A("inv_s", [128, 1], F32)
    combo = A("combo", [128, 1], F32)
    m_b = A("m_b", [128, S], F32)
    rs_b = A("rs_b", [128, S], F32)
    ln_t = A("ln_t", [128, S], F32)
    P = [st.enter_context(nc.psum_tensor(f"P{i}", [128, 1024], F32)) for i in range(4)]
    PK = [f"P{i}" for i in range(4)]
    TSL = [slice(i * 512, (i + 1) * 512) for i in range(NT)]

    def pslot(s):
        ti = s % 4
        half = (s // 4) % 2
        return ti, slice(half * 512, half * 512 + 512)

    def grp_slices(j):
        base = 0 if j == 0 else 2
        out = []
        for t in range(NT):
            out.append((base + t // 2, slice((t % 2) * 512, (t % 2) * 512 + 512)))
        return out

    em.op("DVE", lambda e: e.memset(ones[:], 1.0), writes=["ones"])
    em.dma("SP", "dsc", ssc[:], ss_d, writes=["ssc"])
    em.dma("SP", "dsc", wsc[:], ws_d, writes=["wsc"])
    em.op("DVE", lambda e: e.tensor_copy(s32[:], ssc[:]), reads=["ssc"], writes=["s32"])
    em.op("DVE", lambda e: e.reciprocal(inv_s[:], s32[:]), reads=["s32"], writes=["inv_s"])
    em.op("DVE", lambda e: e.tensor_mul(combo[:], s32[:], wsc[:]), reads=["s32", "wsc"], writes=["combo"])
    em.op("DVE", lambda e: e.tensor_scalar(combo[:], combo[:], 0.5, None, ALU.mult),
          reads=["combo"], writes=["combo"])

    def emit_stats(Rt, Rk, fold_inv_s, ca, sq, cak, sqk, pre=None):
        if pre is not None:
            pre(0)
            if NCH > 1:
                pre(1)
        for c in range(NCH):
            em.op("ACT", lambda e, c=c: e.copy(ca[:], Rt[c][:]), reads=[Rk[c]], writes=[cak])
            em.op("DVE", lambda e: e.tensor_mul(sq[:], ca[:], ca[:]), reads=[cak], writes=[sqk])

            def mm_sum(e, c=c):
                last = None
                for t in range(NT):
                    ti, sl = grp_slices(0)[t]
                    last = e.matmul(P[ti][:, sl], ones[:], ca[:, TSL[t]],
                                    start=(c == 0), stop=(c == NCH - 1))
                return last
            em.op("PE", mm_sum, reads=[cak, "ones"], writes=[PK[0], PK[1]])

            def mm_sq(e, c=c):
                last = None
                for t in range(NT):
                    ti, sl = grp_slices(1)[t]
                    last = e.matmul(P[ti][:, sl], ones[:], sq[:, TSL[t]],
                                    start=(c == 0), stop=(c == NCH - 1))
                return last
            em.op("PE", mm_sq, reads=[sqk, "ones"], writes=[PK[2], PK[3]])
            if pre is not None and c + 2 < NCH:
                pre(c + 2)
        inv_d = 1.0 / D
        for t in range(NT):
            ti, sl = grp_slices(0)[t]
            em.op("DVE", lambda e, t=t, ti=ti, sl=sl:
                  e.tensor_scalar(m_b[:, TSL[t]], P[ti][:, sl], inv_d, None, ALU.mult),
                  reads=[PK[ti]], writes=["m_b"])
            ti2, sl2 = grp_slices(1)[t]
            em.op("DVE", lambda e, t=t, ti2=ti2, sl2=sl2:
                  e.tensor_scalar(ln_t[:, TSL[t]], P[ti2][:, sl2], inv_d, None, ALU.mult),
                  reads=[PK[ti2]], writes=["ln_t"])
        em.op("DVE", lambda e: e.tensor_mul(rs_b[:], m_b[:], m_b[:]), reads=["m_b"], writes=["rs_b"])
        em.op("DVE", lambda e: e.tensor_sub(ln_t[:], ln_t[:], rs_b[:]), reads=["ln_t", "rs_b"], writes=["ln_t"])
        em.op("DVE", lambda e: e.tensor_scalar(ln_t[:], ln_t[:], EPS, None, ALU.add),
              reads=["ln_t"], writes=["ln_t"])
        em.op("ACT", lambda e: e.activation(ln_t[:], ln_t[:], ACTF.Ln), reads=["ln_t"], writes=["ln_t"])
        em.op("ACT", lambda e: e.activation(rs_b[:], ln_t[:], ACTF.Exp, scale=-0.5),
              reads=["ln_t"], writes=["rs_b"])
        if fold_inv_s:
            em.op("DVE", lambda e: e.tensor_scalar(rs_b[:], rs_b[:], inv_s[:], None, ALU.mult),
                  reads=["rs_b", "inv_s"], writes=["rs_b"])

    def emit_ffn(Rt, Rk, xq, xqk, ap, apk, wfb, wfk, wf_sems, to_out=False,
                 post_final=None):
        for c in range(NCH):
            em.op("DVE", lambda e, c=c: e.tensor_sub(ap[:], Rt[c][:], m_b[:]),
                  reads=[Rk[c], "m_b"], writes=[apk])
            em.op("DVE", lambda e: e.tensor_mul(ap[:], ap[:], rs_b[:]),
                  reads=[apk, "rs_b"], writes=[apk])
            em.op("POOL", lambda e: e.tensor_scalar(ap[:], ap[:], FP8MAX, -FP8MAX,
                                                    ALU.min, ALU.max),
                  reads=[apk], writes=[apk])
            em.op("ACT", lambda e, c=c: e.copy(xq[c // 2][:, c % 2, :], ap[:]),
                  reads=[apk], writes=[xqk[c]])
        for o in range(NCH):
            j = o % 2
            em.dma("SP", wf_sems[j], wfb[j][:], wf_d[o], writes=[wfk[j]])
            sls = grp_slices(j)
            gk = sorted({PK[ti] for ti, _ in sls})

            def mms(e, j=j, sls=sls):
                last = None
                for i in range(NCH // 2):
                    for t in range(NT):
                        ti, sl = sls[t]
                        last = e.matmul(P[ti][:, sl],
                                        wfb[j][:, 2 * i * 128:(2 * i + 2) * 128]
                                        .rearrange("p (k d) -> p k d", k=2),
                                        xq[i][:, :, TSL[t]],
                                        start=(i == 0), stop=(i == NCH // 2 - 1),
                                        perf_mode=mybir.MatmulPerfMode.DoubleRow)
                return last
            em.op("PE", mms, reads=[wfk[j]] + list(xqk), writes=gk)
            for t in range(NT):
                ti, sl = sls[t]
                if not to_out:
                    em.op("DVE", lambda e, o=o, t=t, ti=ti, sl=sl:
                          e.scalar_tensor_tensor(Rt[o][:, TSL[t]], P[ti][:, sl], combo[:],
                                                 Rt[o][:, TSL[t]], ALU.mult, ALU.add),
                          reads=[PK[ti], "combo", Rk[o]], writes=[Rk[o]])
                else:
                    em.op("DVE", lambda e, o=o, t=t, ti=ti, sl=sl:
                          e.scalar_tensor_tensor(ap[:, TSL[t]], P[ti][:, sl], combo[:],
                                                 Rt[o][:, TSL[t]], ALU.mult, ALU.add),
                          reads=[PK[ti], "combo", Rk[o]], writes=[apk])
            if to_out:
                em.dma("ACT", "dout", out_d[o * 128:(o + 1) * 128, :], ap[:], reads=[apk])
            if post_final is not None:
                post_final(o)

    s1 = ExitStack()
    A1 = lambda name, shape, dtype: s1.enter_context(nc.sbuf_tensor(name, shape, dtype))
    R = [A1(f"R{c}", [128, S], F32) for c in range(NCH)]
    RK = [f"R{c}" for c in range(NCH)]
    ca1 = A1("ca1", [128, S], BF16)
    sq1 = A1("sq1", [128, S], BF16)
    xq1 = [A1(f"xq{c}", [128, 2, S], F8) for c in range(NCH // 2)]
    XQ1K = [f"xq{c}" for c in range(NCH)]
    ap1 = A1("ap1", [128, S], F32)
    wfb1 = [A1(f"wfb{j}", [128, NCH * 128], F8) for j in range(2)]
    WF1K = ["wfb0", "wfb1"]

    def pre_x(c):
        q = "SP" if c % 3 else "POOL"
        em.dma(q, f"dxi{c}", R[c][:], x_d[c * 128:(c + 1) * 128, :], writes=[RK[c]])
    emit_stats(R, RK, True, ca1, sq1, "ca1", "sq1", pre=pre_x)

    def spill_r(o):
        em.dma("ACT", "dsp", spR_d[o], R[o][:], reads=[RK[o]], writes=[f"spR{o}"])
    emit_ffn(R, RK, xq1, XQ1K, ap1, "ap1", wfb1, WF1K, ("wf1A", "wf1B"),
             post_final=spill_r)
    emit_stats(R, RK, False, ca1, sq1, "ca1", "sq1")
    em.free_regions(RK + XQ1K + WF1K + ["ca1", "sq1", "ap1"])
    s1.close()

    s2 = ExitStack()
    A2 = lambda name, shape, dtype: s2.enter_context(nc.sbuf_tensor(name, shape, dtype))
    u = [A2(f"u{c}", [128, S], BF16) for c in range(NCH)]
    UK = [f"u{c}" for c in range(NCH)]
    ln2 = [A2(f"ln2_{c}", [128, S], BF16) for c in range(NCH)]
    LNK = [f"ln2_{c}" for c in range(NCH)]

    s2a = ExitStack()
    rb = [s2a.enter_context(nc.sbuf_tensor(f"rb{j}", [128, S], F32)) for j in range(2)]
    RBK = ["rb0", "rb1"]
    for c in range(NCH):
        j = c % 2
        em.dma("SP", f"dr{j}", rb[j][:], spR_d[c], reads=[f"spR{c}"], writes=[RBK[j]])
        em.op("DVE", lambda e, j=j: e.tensor_sub(rb[j][:], rb[j][:], m_b[:]),
              reads=[RBK[j], "m_b"], writes=[RBK[j]])
        em.op("DVE", lambda e, c=c, j=j: e.tensor_mul(ln2[c][:], rb[j][:], rs_b[:]),
              reads=[RBK[j], "rs_b"], writes=[LNK[c]])
    em.free_regions(RBK)
    s2a.close()

    s2b = ExitStack()
    A2b = lambda name, shape, dtype: s2b.enter_context(nc.sbuf_tensor(name, shape, dtype))
    qs2 = [A2b(f"qs{j}", [128, S], BF16) for j in range(2)]
    ks2 = [A2b(f"ks{j}", [128, S], BF16) for j in range(2)]
    QSK = ["qs0", "qs1"]
    KSK = ["ks0", "ks1"]
    vs = A2b("vs", [128, S], BF16)
    es2 = [A2b(f"es{j}", [128, S], BF16) for j in range(3)]
    ESK = ["es0", "es1", "es2"]
    whb = [A2b(f"whb{j}", [128, 3 * NCH * 128], BF16) for j in range(2)]
    WHK = ["whb0", "whb1"]
    bias2 = [A2b(f"bias{j}", [128, 1], F32) for j in range(2)]
    BIK = ["bias0", "bias1"]
    am2 = [A2b(f"am{j}", [128, 2], BF16) for j in range(2)]
    AMK = ["am0", "am1"]
    mm2 = [A2b(f"mm{j}", [128, 2], F32) for j in range(2)]
    MMK = ["mm0", "mm1"]
    rp2 = [A2b(f"rp{j}", [128, 2], F32) for j in range(3)]
    RPK = ["rp0", "rp1", "rp2"]
    rv2 = [A2b(f"rv{j}", [128, 1], F32) for j in range(3)]
    RVK = ["rv0", "rv1", "rv2"]
    rl2 = [A2b(f"rl{j}", [128, 128], BF16) for j in range(3)]
    RLK = ["rl0", "rl1", "rl2"]

    slot_ctr = [0]

    def next_slot():
        s = slot_ctr[0]
        slot_ctr[0] += 1
        return pslot(s)

    def emit_qkv_mm(g, dst, dstk, whk, jb, pieces=False):
        for t in range(NT):
            ti, sl = next_slot()
            if pieces:
                for i in range(NCH):
                    def piece(e, t=t, g=g, jb=jb, ti=ti, sl=sl, i=i):
                        return e.matmul(P[ti][:, sl],
                                        whb[jb][:, (g * NCH + i) * 128:(g * NCH + i + 1) * 128],
                                        ln2[i][:, TSL[t]], start=(i == 0), stop=(i == NCH - 1))
                    em.op("PE", piece, reads=[whk, LNK[i]], writes=[PK[ti]])
            else:
                def mms(e, t=t, g=g, jb=jb, ti=ti, sl=sl):
                    last = None
                    for i in range(NCH):
                        last = e.matmul(P[ti][:, sl],
                                        whb[jb][:, (g * NCH + i) * 128:(g * NCH + i + 1) * 128],
                                        ln2[i][:, TSL[t]], start=(i == 0), stop=(i == NCH - 1))
                    return last
                em.op("PE", mms, reads=[whk] + LNK, writes=[PK[ti]])
            em.op("DVE", lambda e, t=t, ti=ti, sl=sl, dst=dst:
                  e.tensor_copy(dst[:, TSL[t]], P[ti][:, sl]),
                  reads=[PK[ti]], writes=[dstk])

    HSL = [slice(0, HW), slice(HW, S)]

    def emit_scores_half(qc, hf, jb):
        def mms(e, qc=qc, hf=hf, jb=jb):
            last = None
            off = HSL[hf].start
            n = 0
            while n < HW:
                w = min(512, HW - n)
                last = e.matmul(P[hf][:, n:n + w], qs2[jb][:, qc * 128:(qc + 1) * 128],
                                ks2[jb][:, off + n:off + n + w], start=True, stop=True)
                n += w
            return last
        em.op("PE", mms, reads=[QSK[jb], KSK[jb]], writes=[PK[hf]])

    def emit_colsum(qc, je):
        def mms(e, qc=qc, je=je):
            last = None
            for t in range(NT):
                ti, sl = grp_slices(1)[t]
                last = e.matmul(P[ti][:, sl], rl2[je][:], es2[je][:, TSL[t]],
                                start=(qc == 0), stop=(qc == QC - 1))
            return last
        em.op("PE", mms, reads=[RLK[je], ESK[je] + "_h0", ESK[je] + "_h1"],
              writes=[PK[2], PK[3]])

    for h in range(H):
        jb = h % 2
        em.dma("POOL", f"wh{jb}", whb[jb][:], wqkv_d[h], writes=[WHK[jb]])
        slot_ctr[0] = 0
        emit_qkv_mm(0, qs2[jb], QSK[jb], WHK[jb], jb)
        em.op("DVE", lambda e, jb=jb: e.tensor_reduce(am2[jb][:, 0:1], qs2[jb][:], AX.X,
                                                      ALU.max, apply_absolute_value=True),
              reads=[QSK[jb]], writes=[AMK[jb]])
        emit_qkv_mm(1, ks2[jb], KSK[jb], WHK[jb], jb)
        em.op("DVE", lambda e, jb=jb: e.tensor_reduce(am2[jb][:, 1:2], ks2[jb][:], AX.X,
                                                      ALU.max, apply_absolute_value=True),
              reads=[KSK[jb]], writes=[AMK[jb]])
        em.op("DVE", lambda e, jb=jb: e.tensor_mul(am2[jb][:], am2[jb][:], am2[jb][:]),
              reads=[AMK[jb]], writes=[AMK[jb]])
        emit_qkv_mm(2, vs, "vs", WHK[jb], jb)
        em.op("PE", lambda e, jb=jb: e.matmul(P[3][:, 512:514], ones[:], am2[jb][:],
                                              start=True, stop=True),
              reads=[AMK[jb], "ones"], writes=[PK[3]])
        em.op("DVE", lambda e, jb=jb: e.tensor_copy(mm2[jb][:], P[3][:, 512:514]),
              reads=[PK[3]], writes=[MMK[jb]])
        em.op("DVE", lambda e, jb=jb: e.tensor_mul(mm2[jb][:, 0:1], mm2[jb][:, 0:1],
                                                   mm2[jb][:, 1:2]),
              reads=[MMK[jb]], writes=[MMK[jb]])
        em.op("ACT", lambda e, jb=jb: e.activation(mm2[jb][:, 0:1], mm2[jb][:, 0:1], ACTF.Ln),
              reads=[MMK[jb]], writes=[MMK[jb]])
        em.op("ACT", lambda e, jb=jb: e.activation(bias2[jb][:], mm2[jb][:, 0:1], ACTF.Exp, scale=0.5),
              reads=[MMK[jb]], writes=[BIK[jb]])
        em.op("DVE", lambda e, jb=jb: e.tensor_scalar(bias2[jb][:], bias2[jb][:], -ISQ, -BCLAMP,
                                                      ALU.mult, ALU.max),
              reads=[BIK[jb]], writes=[BIK[jb]])
        for qc in range(QC):
            je = qc % 3
            emit_scores_half(qc, 0, jb)
            em.op("ACT", lambda e, je=je, jb=jb:
                  e.activation(es2[je][:, HSL[0]], P[0][:, 0:HW], ACTF.Exp,
                               bias=bias2[jb][:], scale=ISQ),
                  reads=[PK[0], BIK[jb]], writes=[ESK[je] + "_h0"])
            emit_scores_half(qc, 1, jb)
            em.op("DVE", lambda e, je=je: e.tensor_reduce(rp2[je][:, 0:1], es2[je][:, HSL[0]],
                                                          AX.X, ALU.add),
                  reads=[ESK[je] + "_h0"], writes=[RPK[je] + "_a"])
            em.op("ACT", lambda e, je=je, jb=jb:
                  e.activation(es2[je][:, HSL[1]], P[1][:, 0:HW], ACTF.Exp,
                               bias=bias2[jb][:], scale=ISQ,
                               accum_out=rp2[je][:, 1:2]),
                  reads=[PK[1], BIK[jb]], writes=[ESK[je] + "_h1", RPK[je] + "_b"])
            em.op("DVE", lambda e, je=je: e.tensor_reduce(rv2[je][:], rp2[je][:], AX.X, ALU.add),
                  reads=[RPK[je] + "_a", RPK[je] + "_b"], writes=[RVK[je]])
            em.op("DVE", lambda e, je=je: e.reciprocal(rv2[je][:], rv2[je][:]),
                  reads=[RVK[je]], writes=[RVK[je]])
            em.op("DVE", lambda e, je=je: e.tensor_scalar(rl2[je][:], ones[:], rv2[je][:],
                                                          None, ALU.mult),
                  reads=["ones", RVK[je]], writes=[RLK[je]])
            if qc >= 1:
                emit_colsum(qc - 1, (qc - 1) % 3)
        emit_colsum(QC - 1, (QC - 1) % 3)
        for p in range((NT + 1) // 2):
            w = min(1024, S - p * 1024)
            em.op("DVE", lambda e, h=h, p=p, w=w:
                  e.tensor_mul(u[h][:, p * 1024:p * 1024 + w], vs[:, p * 1024:p * 1024 + w],
                               P[2 + p][:, 0:w]),
                  reads=["vs", PK[2 + p]], writes=[UK[h]])
    em.free_regions(QSK + KSK + [k + s for k in ESK for s in ("_h0", "_h1")] +
                    WHK + BIK + [k + s for k in RPK for s in ("_a", "_b")] + RVK + RLK +
                    AMK + MMK + ["vs"])
    s2b.close()

    s3 = ExitStack()
    A3 = lambda name, shape, dtype: s3.enter_context(nc.sbuf_tensor(name, shape, dtype))
    wob = [A3(f"wob{j}", [128, NCH * 128], BF16) for j in range(2)]
    WOK = ["wob0", "wob1"]
    rr = [A3(f"rr{j}", [128, S], F32) for j in range(2)]
    RRK = ["rr0", "rr1"]
    stg = [A3(f"stg{j}", [128, S], F32) for j in range(2)]
    STK = ["stg0", "stg1"]
    for o in range(NCH):
        j = o % 2
        em.dma("POOL", f"dwo{j}", wob[j][:], wo_d[o], writes=[WOK[j]])
        em.dma("SP", f"drr{j}", rr[j][:], spR_d[o], reads=[f"spR{o}"], writes=[RRK[j]])
        sls = grp_slices(j)

        def mms(e, j=j, sls=sls):
            last = None
            for hd in range(NCH):
                for t in range(NT):
                    ti, sl = sls[t]
                    last = e.matmul(P[ti][:, sl], wob[j][:, hd * 128:(hd + 1) * 128],
                                    u[hd][:, TSL[t]], start=(hd == 0), stop=(hd == NCH - 1))
            return last
        em.op("PE", mms, reads=[WOK[j]] + UK, writes=sorted({PK[ti] for ti, _ in sls}))
        for t in range(NT):
            ti, sl = sls[t]
            em.op("DVE", lambda e, t=t, j=j, ti=ti, sl=sl:
                  e.tensor_add(stg[j][:, TSL[t]], P[ti][:, sl], rr[j][:, TSL[t]]),
                  reads=[PK[ti], RRK[j]], writes=[STK[j]])
        em.dma("ACT", "dx3", spX_d[o], stg[j][:], reads=[STK[j]], writes=[f"spX{o}"])
    em.free_regions(WOK + RRK + STK)
    s3.close()
    em.free_regions(UK + LNK)
    s2.close()

    s4 = ExitStack()
    A4 = lambda name, shape, dtype: s4.enter_context(nc.sbuf_tensor(name, shape, dtype))
    R2 = [A4(f"R2_{c}", [128, S], F32) for c in range(NCH)]
    R2K = [f"R2_{c}" for c in range(NCH)]
    ca2 = A4("ca2", [128, S], BF16)
    sq2 = A4("sq2", [128, S], BF16)
    xq2 = [A4(f"xr{c}", [128, 2, S], F8) for c in range(NCH // 2)]
    XQ2K = [f"xr{c}" for c in range(NCH)]
    ap2 = A4("ap2", [128, S], F32)
    wfb2 = [A4(f"wg{j}", [128, NCH * 128], F8) for j in range(2)]
    WF2K = ["wg0", "wg1"]

    def pre_x3(c):
        em.dma("SP", f"dxr{c}", R2[c][:], spX_d[c], reads=[f"spX{c}"], writes=[R2K[c]])
    emit_stats(R2, R2K, True, ca2, sq2, "ca2", "sq2", pre=pre_x3)
    emit_ffn(R2, R2K, xq2, XQ2K, ap2, "ap2", wfb2, WF2K, ("wf2A", "wf2B"),
             to_out=True)
    em.free_regions(R2K + XQ2K + WF2K + ["ca2", "sq2", "ap2"])
    s4.close()

    out_sem = em.sems["dout"]
    n_out = em.cnt["dout"]
    em.raw("SP", lambda e: e.wait_ge(out_sem, 16 * n_out))

    if verbose:
        print("sbuf bytes remaining:", nc.sbuf_bytes_remaining)
    em.finalize()
    st.close()
    return nc


def _host_layout(x_b, w_qkv, w_o, w_ffn, w_scale, in_scale, D, S):
    NCH = D // 128
    H = NCH
    wqkv = np.ascontiguousarray(
        w_qkv.reshape(NCH, 128, 3, H, 128).transpose(3, 1, 2, 0, 4).reshape(H, 128, 3 * NCH * 128))
    wo = np.ascontiguousarray(
        w_o.reshape(NCH, 128, NCH, 128).transpose(2, 1, 0, 3).reshape(NCH, 128, NCH * 128))
    wf = np.ascontiguousarray(
        w_ffn.reshape(NCH, 128, NCH, 128).transpose(0, 3, 2, 1).reshape(NCH, 128, NCH * 128))
    wf = wf.view(ml_dtypes.float8_e4m3)
    ss = np.ascontiguousarray(np.broadcast_to(np.asarray(in_scale).reshape(1, 1), (128, 1)))
    ws = np.ascontiguousarray(np.broadcast_to(np.asarray(w_scale).reshape(1, 1), (128, 1)))
    return {"x": np.ascontiguousarray(x_b.T), "wqkv": wqkv, "wo": wo, "wf": wf,
            "sscale": ss, "wscale": ws}


def kernel(L0_attn_x, L0_attn_w_qkv, L0_attn_w_o, L0_ffn_weight,
           L0_ffn_weight_scale, L0_ffn_input_scale):
    D, S = 2048, 2048
    x = np.asarray(L0_attn_x)
    w_qkv = np.asarray(L0_attn_w_qkv)
    w_o = np.asarray(L0_attn_w_o)
    w_ffn = np.asarray(L0_ffn_weight)
    nc = build_program(D, S)
    in_maps = [
        _host_layout(x[b], w_qkv, w_o, w_ffn, L0_ffn_weight_scale, L0_ffn_input_scale, D, S)
        for b in range(8)
    ]
    res = run_bass_kernel_spmd(nc, in_maps, core_ids=list(range(8)))
    out = np.stack([np.asarray(r["out"]).T for r in res.results], axis=0)
    return out.astype(np.float32)


# version 23 (final)
# speedup vs baseline: 1.2021x; 1.0207x over previous
import math
from contextlib import ExitStack

import numpy as np
import ml_dtypes

import concourse.bass as bass
import concourse.mybir as mybir
from concourse.bass_utils import run_bass_kernel_spmd

F32 = mybir.dt.float32
BF16 = mybir.dt.bfloat16
F8 = mybir.dt.float8e4
ALU = mybir.AluOpType
ACTF = mybir.ActivationFunctionType
AX = mybir.AxisListType

EPS = 1e-5
FP8MAX = 448.0
ISQ = 1.0 / math.sqrt(128.0)
BCLAMP = 44.0

ENGINES = ("PE", "ACT", "DVE", "SP", "POOL")


class Em:

    def __init__(self, nc):
        self.nc = nc
        self.sems = {}
        self.cnt = {}
        self.clock = {e: {} for e in ENGINES}
        self.snap = {}
        self.lw = {}
        self.rd = {}
        self.stream = {e: [] for e in ENGINES}
        self.fence = {}
        for e in ENGINES:
            self._sem(e)

    def _sem(self, dim):
        if dim not in self.sems:
            self.sems[dim] = self.nc.alloc_semaphore("em_" + dim)
            self.cnt[dim] = 0
        return self.sems[dim]

    def _waits(self, eng, reads, writes):
        need = {}

        def add(dt):
            if dt is None:
                return
            d, t = dt
            if need.get(d, 0) < t:
                need[d] = t

        for r in reads:
            add(self.lw.get(r))
        for w in writes:
            add(self.lw.get(w))
            for d, t in self.rd.get(w, {}).items():
                add((d, t))
        for d, t in self.fence.items():
            add((d, t))
        out = []
        c = self.clock[eng]
        for d in sorted(need):
            t = need[d]
            if c.get(d, 0) >= t:
                continue
            if d in ENGINES:
                st = t
                val = t
            else:
                st = self.cnt[d]
                val = 16 * st
            sn = self.snap.get((d, st), {})
            for d2, t2 in sn.items():
                if c.get(d2, 0) < t2:
                    c[d2] = t2
            if c.get(d, 0) < st:
                c[d] = st
            out.append((self.sems[d], val))
        return out

    def _update(self, reads, writes, dim, t):
        for r in reads:
            self.rd.setdefault(r, {})[dim] = t
        for w in writes:
            self.lw[w] = (dim, t)
            self.rd[w] = {}

    def op(self, eng, fn, reads=(), writes=()):
        waits = self._waits(eng, reads, writes)
        self.cnt[eng] += 1
        t = self.cnt[eng]
        self.snap[(eng, t)] = dict(self.clock[eng], **{eng: t})
        self._update(reads, writes, eng, t)
        sem = self.sems[eng]

        def emit(e, waits=waits, fn=fn, sem=sem):
            for s, v in waits:
                e.wait_ge(s, v)
            fn(e).then_inc(sem, 1)

        self.stream[eng].append(emit)
        return t

    def dma(self, queue, sdim, out_ap, in_ap, reads=(), writes=()):
        self._sem(sdim)
        waits = self._waits(queue, reads, writes)
        self.cnt[sdim] += 1
        t = self.cnt[sdim]
        self.snap[(sdim, t)] = dict(self.clock[queue], **{sdim: t})
        self._update(reads, writes, sdim, t)
        sem = self.sems[sdim]

        def emit(e, waits=waits, out_ap=out_ap, in_ap=in_ap, sem=sem):
            for s, v in waits:
                e.wait_ge(s, v)
            e.dma_start(out=out_ap, in_=in_ap).then_inc(sem, 16)

        self.stream[queue].append(emit)
        return t

    def free_regions(self, keys):
        for k in keys:
            dt = self.lw.pop(k, None)
            if dt is not None:
                d, t = dt
                if self.fence.get(d, 0) < t:
                    self.fence[d] = t
            for d, t in self.rd.pop(k, {}).items():
                if self.fence.get(d, 0) < t:
                    self.fence[d] = t

    def raw(self, eng, fn):
        self.stream[eng].append(fn)

    def finalize(self):
        with self.nc.Block() as block:
            for name, reg in (("SP", block.sync), ("PE", block.tensor),
                              ("DVE", block.vector), ("ACT", block.scalar),
                              ("POOL", block.gpsimd)):
                fns = self.stream[name]
                if fns:
                    reg(lambda e, fns=fns: [f(e) for f in fns])


def build_program(D=2048, S=2048, verbose=False):
    NCH = D // 128
    H = NCH
    NT = S // 512
    QC = S // 128
    HW = S // 2
    nc = bass.Bass("TRN2", target_bir_lowering=False, dynamic_dma_scratch_size=8192)
    em = Em(nc)

    x_d = nc.dram_tensor("x", [D, S], F32, kind="ExternalInput").ap()
    wqkv_d = nc.dram_tensor("wqkv", [H, 128, 3 * NCH * 128], F32, kind="ExternalInput").ap()
    wo_d = nc.dram_tensor("wo", [NCH, 128, NCH * 128], F32, kind="ExternalInput").ap()
    wf_d = nc.dram_tensor("wf", [NCH, 128, NCH * 128], F8, kind="ExternalInput").ap()
    ss_d = nc.dram_tensor("sscale", [128, 1], BF16, kind="ExternalInput").ap()
    ws_d = nc.dram_tensor("wscale", [128, 1], F32, kind="ExternalInput").ap()
    out_d = nc.dram_tensor("out", [D, S], F32, kind="ExternalOutput").ap()
    spR_d = nc.dram_tensor("spill_r", [NCH, 128, S], F32, kind="Internal").ap()
    spX_d = nc.dram_tensor("spill_x3", [NCH, 128, S], F32, kind="Internal").ap()

    st = ExitStack()
    A = lambda name, shape, dtype: st.enter_context(nc.sbuf_tensor(name, shape, dtype))

    ones = A("ones", [128, 128], BF16)
    ssc = A("ssc", [128, 1], BF16)
    wsc = A("wsc", [128, 1], F32)
    s32 = A("s32", [128, 1], F32)
    inv_s = A("inv_s", [128, 1], F32)
    combo = A("combo", [128, 1], F32)
    m_b = A("m_b", [128, S], F32)
    rs_b = A("rs_b", [128, S], F32)
    ln_t = A("ln_t", [128, S], F32)
    P = [st.enter_context(nc.psum_tensor(f"P{i}", [128, 1024], F32)) for i in range(4)]
    PK = [f"P{i}" for i in range(4)]
    TSL = [slice(i * 512, (i + 1) * 512) for i in range(NT)]

    def pslot(s):
        ti = s % 4
        half = (s // 4) % 2
        return ti, slice(half * 512, half * 512 + 512)

    def grp_slices(j):
        base = 0 if j == 0 else 2
        out = []
        for t in range(NT):
            out.append((base + t // 2, slice((t % 2) * 512, (t % 2) * 512 + 512)))
        return out

    em.op("DVE", lambda e: e.memset(ones[:], 1.0), writes=["ones"])
    em.dma("SP", "dsc", ssc[:], ss_d, writes=["ssc"])
    em.dma("SP", "dsc", wsc[:], ws_d, writes=["wsc"])
    em.op("DVE", lambda e: e.tensor_copy(s32[:], ssc[:]), reads=["ssc"], writes=["s32"])
    em.op("DVE", lambda e: e.reciprocal(inv_s[:], s32[:]), reads=["s32"], writes=["inv_s"])
    em.op("DVE", lambda e: e.tensor_mul(combo[:], s32[:], wsc[:]), reads=["s32", "wsc"], writes=["combo"])
    em.op("DVE", lambda e: e.tensor_scalar(combo[:], combo[:], 0.5, None, ALU.mult),
          reads=["combo"], writes=["combo"])

    def emit_stats(Rt, Rk, fold_inv_s, ca, sq, cak, sqk, pre=None):
        if pre is not None:
            pre(0)
            if NCH > 1:
                pre(1)
        for c in range(NCH):
            em.op("ACT", lambda e, c=c: e.copy(ca[:], Rt[c][:]), reads=[Rk[c]], writes=[cak])
            em.op("DVE", lambda e: e.tensor_mul(sq[:], ca[:], ca[:]), reads=[cak], writes=[sqk])

            def mm_sum(e, c=c):
                last = None
                for t in range(NT):
                    ti, sl = grp_slices(0)[t]
                    last = e.matmul(P[ti][:, sl], ones[:], ca[:, TSL[t]],
                                    start=(c == 0), stop=(c == NCH - 1))
                return last
            em.op("PE", mm_sum, reads=[cak, "ones"], writes=[PK[0], PK[1]])

            def mm_sq(e, c=c):
                last = None
                for t in range(NT):
                    ti, sl = grp_slices(1)[t]
                    last = e.matmul(P[ti][:, sl], ones[:], sq[:, TSL[t]],
                                    start=(c == 0), stop=(c == NCH - 1))
                return last
            em.op("PE", mm_sq, reads=[sqk, "ones"], writes=[PK[2], PK[3]])
            if pre is not None and c + 2 < NCH:
                pre(c + 2)
        inv_d = 1.0 / D
        for t in range(NT):
            ti, sl = grp_slices(0)[t]
            em.op("DVE", lambda e, t=t, ti=ti, sl=sl:
                  e.tensor_scalar(m_b[:, TSL[t]], P[ti][:, sl], inv_d, None, ALU.mult),
                  reads=[PK[ti]], writes=["m_b"])
            ti2, sl2 = grp_slices(1)[t]
            em.op("DVE", lambda e, t=t, ti2=ti2, sl2=sl2:
                  e.tensor_scalar(ln_t[:, TSL[t]], P[ti2][:, sl2], inv_d, None, ALU.mult),
                  reads=[PK[ti2]], writes=["ln_t"])
        em.op("DVE", lambda e: e.tensor_mul(rs_b[:], m_b[:], m_b[:]), reads=["m_b"], writes=["rs_b"])
        em.op("DVE", lambda e: e.tensor_sub(ln_t[:], ln_t[:], rs_b[:]), reads=["ln_t", "rs_b"], writes=["ln_t"])
        em.op("DVE", lambda e: e.tensor_scalar(ln_t[:], ln_t[:], EPS, None, ALU.add),
              reads=["ln_t"], writes=["ln_t"])
        em.op("ACT", lambda e: e.activation(ln_t[:], ln_t[:], ACTF.Ln), reads=["ln_t"], writes=["ln_t"])
        em.op("ACT", lambda e: e.activation(rs_b[:], ln_t[:], ACTF.Exp, scale=-0.5),
              reads=["ln_t"], writes=["rs_b"])
        if fold_inv_s:
            em.op("DVE", lambda e: e.tensor_scalar(rs_b[:], rs_b[:], inv_s[:], None, ALU.mult),
                  reads=["rs_b", "inv_s"], writes=["rs_b"])

    def emit_ffn(Rt, Rk, xq, xqk, ap, apk, wfb, wfk, wf_sems, to_out=False,
                 post_final=None):
        for c in range(NCH):
            em.op("DVE", lambda e, c=c: e.tensor_sub(ap[:], Rt[c][:], m_b[:]),
                  reads=[Rk[c], "m_b"], writes=[apk])
            em.op("DVE", lambda e: e.tensor_mul(ap[:], ap[:], rs_b[:]),
                  reads=[apk, "rs_b"], writes=[apk])
            em.op("POOL", lambda e: e.tensor_scalar(ap[:], ap[:], FP8MAX, -FP8MAX,
                                                    ALU.min, ALU.max),
                  reads=[apk], writes=[apk])
            em.op("ACT", lambda e, c=c: e.copy(xq[c // 2][:, c % 2, :], ap[:]),
                  reads=[apk], writes=[xqk[c]])
        for o in range(NCH):
            j = o % 2
            em.dma("SP", wf_sems[j], wfb[j][:], wf_d[o], writes=[wfk[j]])
            sls = grp_slices(j)
            gk = sorted({PK[ti] for ti, _ in sls})

            def mms(e, j=j, sls=sls):
                last = None
                for i in range(NCH // 2):
                    for t in range(NT):
                        ti, sl = sls[t]
                        last = e.matmul(P[ti][:, sl],
                                        wfb[j][:, 2 * i * 128:(2 * i + 2) * 128]
                                        .rearrange("p (k d) -> p k d", k=2),
                                        xq[i][:, :, TSL[t]],
                                        start=(i == 0), stop=(i == NCH // 2 - 1),
                                        perf_mode=mybir.MatmulPerfMode.DoubleRow)
                return last
            em.op("PE", mms, reads=[wfk[j]] + list(xqk), writes=gk)
            for t in range(NT):
                ti, sl = sls[t]
                if not to_out:
                    em.op("DVE", lambda e, o=o, t=t, ti=ti, sl=sl:
                          e.scalar_tensor_tensor(Rt[o][:, TSL[t]], P[ti][:, sl], combo[:],
                                                 Rt[o][:, TSL[t]], ALU.mult, ALU.add),
                          reads=[PK[ti], "combo", Rk[o]], writes=[Rk[o]])
                else:
                    em.op("DVE", lambda e, o=o, t=t, ti=ti, sl=sl:
                          e.scalar_tensor_tensor(ap[:, TSL[t]], P[ti][:, sl], combo[:],
                                                 Rt[o][:, TSL[t]], ALU.mult, ALU.add),
                          reads=[PK[ti], "combo", Rk[o]], writes=[apk])
            if to_out:
                em.dma("ACT", "dout", out_d[o * 128:(o + 1) * 128, :], ap[:], reads=[apk])
            if post_final is not None:
                post_final(o)

    s1 = ExitStack()
    A1 = lambda name, shape, dtype: s1.enter_context(nc.sbuf_tensor(name, shape, dtype))
    R = [A1(f"R{c}", [128, S], F32) for c in range(NCH)]
    RK = [f"R{c}" for c in range(NCH)]
    ca1 = A1("ca1", [128, S], BF16)
    sq1 = A1("sq1", [128, S], BF16)
    xq1 = [A1(f"xq{c}", [128, 2, S], F8) for c in range(NCH // 2)]
    XQ1K = [f"xq{c}" for c in range(NCH)]
    ap1 = A1("ap1", [128, S], F32)
    wfb1 = [A1(f"wfb{j}", [128, NCH * 128], F8) for j in range(2)]
    WF1K = ["wfb0", "wfb1"]

    def pre_x(c):
        em.dma("SP", f"dx{c}", R[c][:], x_d[c * 128:(c + 1) * 128, :], writes=[RK[c]])
    emit_stats(R, RK, True, ca1, sq1, "ca1", "sq1", pre=pre_x)

    def spill_r(o):
        em.dma("ACT", "dsp", spR_d[o], R[o][:], reads=[RK[o]], writes=[f"spR{o}"])
    emit_ffn(R, RK, xq1, XQ1K, ap1, "ap1", wfb1, WF1K, ("wf1A", "wf1B"),
             post_final=spill_r)
    emit_stats(R, RK, False, ca1, sq1, "ca1", "sq1")
    em.free_regions(RK + XQ1K + WF1K + ["ca1", "sq1", "ap1"])
    s1.close()

    s2 = ExitStack()
    A2 = lambda name, shape, dtype: s2.enter_context(nc.sbuf_tensor(name, shape, dtype))
    u = [A2(f"u{c}", [128, S], BF16) for c in range(NCH)]
    UK = [f"u{c}" for c in range(NCH)]
    ln2 = [A2(f"ln2_{c}", [128, S], BF16) for c in range(NCH)]
    LNK = [f"ln2_{c}" for c in range(NCH)]

    s2a = ExitStack()
    rb = [s2a.enter_context(nc.sbuf_tensor(f"rb{j}", [128, S], F32)) for j in range(2)]
    RBK = ["rb0", "rb1"]
    for c in range(NCH):
        j = c % 2
        em.dma("SP", f"dr{j}", rb[j][:], spR_d[c], reads=[f"spR{c}"], writes=[RBK[j]])
        em.op("DVE", lambda e, j=j: e.tensor_sub(rb[j][:], rb[j][:], m_b[:]),
              reads=[RBK[j], "m_b"], writes=[RBK[j]])
        em.op("DVE", lambda e, c=c, j=j: e.tensor_mul(ln2[c][:], rb[j][:], rs_b[:]),
              reads=[RBK[j], "rs_b"], writes=[LNK[c]])
    em.free_regions(RBK)
    s2a.close()

    s2b = ExitStack()
    A2b = lambda name, shape, dtype: s2b.enter_context(nc.sbuf_tensor(name, shape, dtype))
    qs2 = [A2b(f"qs{j}", [128, S], BF16) for j in range(2)]
    ks2 = [A2b(f"ks{j}", [128, S], BF16) for j in range(2)]
    QSK = ["qs0", "qs1"]
    KSK = ["ks0", "ks1"]
    vs = A2b("vs", [128, S], BF16)
    es2 = [A2b(f"es{j}", [128, S], BF16) for j in range(3)]
    ESK = ["es0", "es1", "es2"]
    whb = [A2b(f"whb{j}", [128, 3 * NCH * 128], BF16) for j in range(2)]
    WHK = ["whb0", "whb1"]
    bias2 = [A2b(f"bias{j}", [128, 1], F32) for j in range(2)]
    BIK = ["bias0", "bias1"]
    am2 = [A2b(f"am{j}", [128, 2], BF16) for j in range(2)]
    AMK = ["am0", "am1"]
    mm2 = [A2b(f"mm{j}", [128, 2], F32) for j in range(2)]
    MMK = ["mm0", "mm1"]
    rp2 = [A2b(f"rp{j}", [128, 2], F32) for j in range(3)]
    RPK = ["rp0", "rp1", "rp2"]
    rv2 = [A2b(f"rv{j}", [128, 1], F32) for j in range(3)]
    RVK = ["rv0", "rv1", "rv2"]
    rl2 = [A2b(f"rl{j}", [128, 128], BF16) for j in range(3)]
    RLK = ["rl0", "rl1", "rl2"]

    slot_ctr = [0]

    def next_slot():
        s = slot_ctr[0]
        slot_ctr[0] += 1
        return pslot(s)

    def emit_qkv_mm(g, dst, dstk, whk, jb, pieces=False, trange=None):
        for t in (trange if trange is not None else range(NT)):
            ti, sl = next_slot()
            if pieces:
                for i in range(NCH):
                    def piece(e, t=t, g=g, jb=jb, ti=ti, sl=sl, i=i):
                        return e.matmul(P[ti][:, sl],
                                        whb[jb][:, (g * NCH + i) * 128:(g * NCH + i + 1) * 128],
                                        ln2[i][:, TSL[t]], start=(i == 0), stop=(i == NCH - 1))
                    em.op("PE", piece, reads=[whk, LNK[i]], writes=[PK[ti]])
            else:
                def mms(e, t=t, g=g, jb=jb, ti=ti, sl=sl):
                    last = None
                    for i in range(NCH):
                        last = e.matmul(P[ti][:, sl],
                                        whb[jb][:, (g * NCH + i) * 128:(g * NCH + i + 1) * 128],
                                        ln2[i][:, TSL[t]], start=(i == 0), stop=(i == NCH - 1))
                    return last
                em.op("PE", mms, reads=[whk] + LNK, writes=[PK[ti]])
            em.op("DVE", lambda e, t=t, ti=ti, sl=sl, dst=dst:
                  e.tensor_copy(dst[:, TSL[t]], P[ti][:, sl]),
                  reads=[PK[ti]], writes=[dstk])

    HSL = [slice(0, HW), slice(HW, S)]

    def emit_scores_half(qc, hf, jb):
        def mms(e, qc=qc, hf=hf, jb=jb):
            last = None
            off = HSL[hf].start
            n = 0
            while n < HW:
                w = min(512, HW - n)
                last = e.matmul(P[hf][:, n:n + w], qs2[jb][:, qc * 128:(qc + 1) * 128],
                                ks2[jb][:, off + n:off + n + w], start=True, stop=True)
                n += w
            return last
        em.op("PE", mms, reads=[QSK[jb], KSK[jb]], writes=[PK[hf]])

    def emit_colsum(qc, je):
        def mms(e, qc=qc, je=je):
            last = None
            for t in range(NT):
                ti, sl = grp_slices(1)[t]
                last = e.matmul(P[ti][:, sl], rl2[je][:], es2[je][:, TSL[t]],
                                start=(qc == 0), stop=(qc == QC - 1))
            return last
        em.op("PE", mms, reads=[RLK[je], ESK[je] + "_h0", ESK[je] + "_h1"],
              writes=[PK[2], PK[3]])

    for h in range(H):
        jb = h % 2
        em.dma("POOL", f"wh{jb}", whb[jb][:], wqkv_d[h], writes=[WHK[jb]])
        slot_ctr[0] = 0
        emit_qkv_mm(0, qs2[jb], QSK[jb], WHK[jb], jb)
        em.op("DVE", lambda e, jb=jb: e.tensor_reduce(am2[jb][:, 0:1], qs2[jb][:], AX.X,
                                                      ALU.max, apply_absolute_value=True),
              reads=[QSK[jb]], writes=[AMK[jb]])
        emit_qkv_mm(1, ks2[jb], KSK[jb], WHK[jb], jb)
        em.op("DVE", lambda e, jb=jb: e.tensor_reduce(am2[jb][:, 1:2], ks2[jb][:], AX.X,
                                                      ALU.max, apply_absolute_value=True),
              reads=[KSK[jb]], writes=[AMK[jb]])
        em.op("DVE", lambda e, jb=jb: e.tensor_mul(am2[jb][:], am2[jb][:], am2[jb][:]),
              reads=[AMK[jb]], writes=[AMK[jb]])
        emit_qkv_mm(2, vs, "vs", WHK[jb], jb, trange=range(0, max(1, NT // 2)))
        em.op("PE", lambda e, jb=jb: e.matmul(P[3][:, 512:514], ones[:], am2[jb][:],
                                              start=True, stop=True),
              reads=[AMK[jb], "ones"], writes=[PK[3]])
        em.op("DVE", lambda e, jb=jb: e.tensor_copy(mm2[jb][:], P[3][:, 512:514]),
              reads=[PK[3]], writes=[MMK[jb]])
        em.op("DVE", lambda e, jb=jb: e.tensor_mul(mm2[jb][:, 0:1], mm2[jb][:, 0:1],
                                                   mm2[jb][:, 1:2]),
              reads=[MMK[jb]], writes=[MMK[jb]])
        em.op("ACT", lambda e, jb=jb: e.activation(mm2[jb][:, 0:1], mm2[jb][:, 0:1], ACTF.Ln),
              reads=[MMK[jb]], writes=[MMK[jb]])
        em.op("ACT", lambda e, jb=jb: e.activation(bias2[jb][:], mm2[jb][:, 0:1], ACTF.Exp, scale=0.5),
              reads=[MMK[jb]], writes=[BIK[jb]])
        em.op("DVE", lambda e, jb=jb: e.tensor_scalar(bias2[jb][:], bias2[jb][:], -ISQ, -BCLAMP,
                                                      ALU.mult, ALU.max),
              reads=[BIK[jb]], writes=[BIK[jb]])
        if NT > 1:
            emit_qkv_mm(2, vs, "vs", WHK[jb], jb, trange=range(NT // 2, NT))
        for qc in range(QC):
            je = qc % 3
            emit_scores_half(qc, 0, jb)
            em.op("ACT", lambda e, je=je, jb=jb:
                  e.activation(es2[je][:, HSL[0]], P[0][:, 0:HW], ACTF.Exp,
                               bias=bias2[jb][:], scale=ISQ),
                  reads=[PK[0], BIK[jb]], writes=[ESK[je] + "_h0"])
            emit_scores_half(qc, 1, jb)
            em.op("DVE", lambda e, je=je: e.tensor_reduce(rp2[je][:, 0:1], es2[je][:, HSL[0]],
                                                          AX.X, ALU.add),
                  reads=[ESK[je] + "_h0"], writes=[RPK[je] + "_a"])
            em.op("ACT", lambda e, je=je, jb=jb:
                  e.activation(es2[je][:, HSL[1]], P[1][:, 0:HW], ACTF.Exp,
                               bias=bias2[jb][:], scale=ISQ,
                               accum_out=rp2[je][:, 1:2]),
                  reads=[PK[1], BIK[jb]], writes=[ESK[je] + "_h1", RPK[je] + "_b"])
            em.op("DVE", lambda e, je=je: e.tensor_reduce(rv2[je][:], rp2[je][:], AX.X, ALU.add),
                  reads=[RPK[je] + "_a", RPK[je] + "_b"], writes=[RVK[je]])
            em.op("DVE", lambda e, je=je: e.reciprocal(rv2[je][:], rv2[je][:]),
                  reads=[RVK[je]], writes=[RVK[je]])
            em.op("DVE", lambda e, je=je: e.tensor_scalar(rl2[je][:], ones[:], rv2[je][:],
                                                          None, ALU.mult),
                  reads=["ones", RVK[je]], writes=[RLK[je]])
            if qc >= 1:
                emit_colsum(qc - 1, (qc - 1) % 3)
        emit_colsum(QC - 1, (QC - 1) % 3)
        for p in range((NT + 1) // 2):
            w = min(1024, S - p * 1024)
            em.op("DVE", lambda e, h=h, p=p, w=w:
                  e.tensor_mul(u[h][:, p * 1024:p * 1024 + w], vs[:, p * 1024:p * 1024 + w],
                               P[2 + p][:, 0:w]),
                  reads=["vs", PK[2 + p]], writes=[UK[h]])
    em.free_regions(QSK + KSK + [k + s for k in ESK for s in ("_h0", "_h1")] +
                    WHK + BIK + [k + s for k in RPK for s in ("_a", "_b")] + RVK + RLK +
                    AMK + MMK + ["vs"])
    s2b.close()

    s3 = ExitStack()
    A3 = lambda name, shape, dtype: s3.enter_context(nc.sbuf_tensor(name, shape, dtype))
    wob = [A3(f"wob{j}", [128, NCH * 128], BF16) for j in range(2)]
    WOK = ["wob0", "wob1"]
    rr = [A3(f"rr{j}", [128, S], F32) for j in range(2)]
    RRK = ["rr0", "rr1"]
    stg = [A3(f"stg{j}", [128, S], F32) for j in range(2)]
    STK = ["stg0", "stg1"]
    for o in range(NCH):
        j = o % 2
        em.dma("POOL", f"dwo{j}", wob[j][:], wo_d[o], writes=[WOK[j]])
        em.dma("SP", f"drr{j}", rr[j][:], spR_d[o], reads=[f"spR{o}"], writes=[RRK[j]])
        sls = grp_slices(j)

        def mms(e, j=j, sls=sls):
            last = None
            for hd in range(NCH):
                for t in range(NT):
                    ti, sl = sls[t]
                    last = e.matmul(P[ti][:, sl], wob[j][:, hd * 128:(hd + 1) * 128],
                                    u[hd][:, TSL[t]], start=(hd == 0), stop=(hd == NCH - 1))
            return last
        em.op("PE", mms, reads=[WOK[j]] + UK, writes=sorted({PK[ti] for ti, _ in sls}))
        for t in range(NT):
            ti, sl = sls[t]
            em.op("DVE", lambda e, t=t, j=j, ti=ti, sl=sl:
                  e.tensor_add(stg[j][:, TSL[t]], P[ti][:, sl], rr[j][:, TSL[t]]),
                  reads=[PK[ti], RRK[j]], writes=[STK[j]])
        em.dma("ACT", "dx3", spX_d[o], stg[j][:], reads=[STK[j]], writes=[f"spX{o}"])
    em.free_regions(WOK + RRK + STK)
    s3.close()
    em.free_regions(UK + LNK)
    s2.close()

    s4 = ExitStack()
    A4 = lambda name, shape, dtype: s4.enter_context(nc.sbuf_tensor(name, shape, dtype))
    R2 = [A4(f"R2_{c}", [128, S], F32) for c in range(NCH)]
    R2K = [f"R2_{c}" for c in range(NCH)]
    ca2 = A4("ca2", [128, S], BF16)
    sq2 = A4("sq2", [128, S], BF16)
    xq2 = [A4(f"xr{c}", [128, 2, S], F8) for c in range(NCH // 2)]
    XQ2K = [f"xr{c}" for c in range(NCH)]
    ap2 = A4("ap2", [128, S], F32)
    wfb2 = [A4(f"wg{j}", [128, NCH * 128], F8) for j in range(2)]
    WF2K = ["wg0", "wg1"]

    def pre_x3(c):
        em.dma("SP", f"dx3r{c}", R2[c][:], spX_d[c], reads=[f"spX{c}"], writes=[R2K[c]])
    emit_stats(R2, R2K, True, ca2, sq2, "ca2", "sq2", pre=pre_x3)
    emit_ffn(R2, R2K, xq2, XQ2K, ap2, "ap2", wfb2, WF2K, ("wf2A", "wf2B"),
             to_out=True)
    em.free_regions(R2K + XQ2K + WF2K + ["ca2", "sq2", "ap2"])
    s4.close()

    out_sem = em.sems["dout"]
    n_out = em.cnt["dout"]
    em.raw("SP", lambda e: e.wait_ge(out_sem, 16 * n_out))

    if verbose:
        print("sbuf bytes remaining:", nc.sbuf_bytes_remaining)
    em.finalize()
    st.close()
    return nc


def _host_layout(x_b, w_qkv, w_o, w_ffn, w_scale, in_scale, D, S):
    NCH = D // 128
    H = NCH
    wqkv = np.ascontiguousarray(
        w_qkv.reshape(NCH, 128, 3, H, 128).transpose(3, 1, 2, 0, 4).reshape(H, 128, 3 * NCH * 128))
    wo = np.ascontiguousarray(
        w_o.reshape(NCH, 128, NCH, 128).transpose(2, 1, 0, 3).reshape(NCH, 128, NCH * 128))
    wf = np.ascontiguousarray(
        w_ffn.reshape(NCH, 128, NCH, 128).transpose(0, 3, 2, 1).reshape(NCH, 128, NCH * 128))
    wf = wf.view(ml_dtypes.float8_e4m3)
    ss = np.ascontiguousarray(np.broadcast_to(np.asarray(in_scale).reshape(1, 1), (128, 1)))
    ws = np.ascontiguousarray(np.broadcast_to(np.asarray(w_scale).reshape(1, 1), (128, 1)))
    return {"x": np.ascontiguousarray(x_b.T), "wqkv": wqkv, "wo": wo, "wf": wf,
            "sscale": ss, "wscale": ws}


def kernel(L0_attn_x, L0_attn_w_qkv, L0_attn_w_o, L0_ffn_weight,
           L0_ffn_weight_scale, L0_ffn_input_scale):
    D, S = 2048, 2048
    x = np.asarray(L0_attn_x)
    w_qkv = np.asarray(L0_attn_w_qkv)
    w_o = np.asarray(L0_attn_w_o)
    w_ffn = np.asarray(L0_ffn_weight)
    nc = build_program(D, S)
    in_maps = [
        _host_layout(x[b], w_qkv, w_o, w_ffn, L0_ffn_weight_scale, L0_ffn_input_scale, D, S)
        for b in range(8)
    ]
    res = run_bass_kernel_spmd(nc, in_maps, core_ids=list(range(8)))
    out = np.stack([np.asarray(r["out"]).T for r in res.results], axis=0)
    return out.astype(np.float32)
